# version 04
# baseline (speedup 1.0000x reference)
import contextlib
import numpy as np
import ml_dtypes
import concourse.bass as bass
import concourse.mybir as mybir
from concourse.bass_utils import run_bass_kernel_spmd
from concourse.aot_env import allow_ici_all_to_all

F32 = mybir.dt.float32
BF16 = mybir.dt.bfloat16
I8 = mybir.dt.int8
AF = mybir.ActivationFunctionType
ALU = mybir.AluOpType

P = 128
D = 2048
TOK = 1024
TB = 512
NTB = TOK // TB
DCH = D // P
INTER = 8192
JC = INTER // P
JBC = 8
NJB = JC // JBC
NH_OWN = 16
HF = 64
CS = 16
TSEQ = 2048
NCHUNKS = TSEQ // CS
NGRP = NCHUNKS // 8
EPS = 1e-5

STAGE = 5
PAIRS = [[0, 1], [2, 3], [4, 5], [6, 7]]


class Ctr:
    def __init__(self, stack, nc, name):
        self.h = stack.enter_context(nc.semaphore(name))
        self.v = 0

    def inc(self, instr, n=1):
        instr.then_inc(self.h, n)
        self.v += n
        return self.v


class NS:
    pass


def build_program():
    nc = bass.Bass("TRN2", target_bir_lowering=False)

    xT = nc.dram_tensor("xT", [D, TOK], F32, kind="ExternalInput").ap()
    wg_r = nc.dram_tensor("wg_r", [JC, P, DCH, P], I8, kind="ExternalInput").ap()
    wu_r = nc.dram_tensor("wu_r", [JC, P, DCH, P], I8, kind="ExternalInput").ap()
    wd_r = nc.dram_tensor("wd_r", [NJB, DCH, P, JBC, P], I8, kind="ExternalInput").ap()
    sgate = nc.dram_tensor("sgate", [P, JC], F32, kind="ExternalInput").ap()
    sup = nc.dram_tensor("sup", [P, JC], F32, kind="ExternalInput").ap()
    sdown = nc.dram_tensor("sdown", [P, DCH], F32, kind="ExternalInput").ap()
    ones1 = nc.dram_tensor("ones1", [1, P], F32, kind="ExternalInput").ap()
    onescol = nc.dram_tensor("onescol", [P, 1], F32, kind="ExternalInput").ap()
    allones = nc.dram_tensor("allones", [P, HF], F32, kind="ExternalInput").ap()
    out = nc.dram_tensor("outx", [D, TOK], F32, kind="ExternalOutput").ap()

    wqT = nc.dram_tensor("wqT", [D, TOK], F32, kind="ExternalInput").ap()
    wkT = nc.dram_tensor("wkT", [D, TOK], F32, kind="ExternalInput").ap()
    wvT = nc.dram_tensor("wvT", [D, TOK], F32, kind="ExternalInput").ap()
    woT = nc.dram_tensor("woT", [D, D], F32, kind="ExternalInput").ap()
    wilrT = nc.dram_tensor("wilrT", [D, NH_OWN], F32, kind="ExternalInput").ap()
    bilr = nc.dram_tensor("bilr", [1, NH_OWN], F32, kind="ExternalInput").ap()
    w1pack = nc.dram_tensor("w1pack", [P, 8 * HF], F32, kind="ExternalInput").ap()
    ident = nc.dram_tensor("ident", [P, P], F32, kind="ExternalInput").ap()
    amask = nc.dram_tensor("amask", [P, 64], F32, kind="ExternalInput").ap()
    tokmul = nc.dram_tensor("tokmul", [P, 1], F32, kind="ExternalInput").ap()
    selflags = nc.dram_tensor("selflags", [P, 4], F32, kind="ExternalInput").ap()

    d_wq = nc.dram_tensor("d_wq", [2, D, TOK], BF16)
    d_wk = nc.dram_tensor("d_wk", [2, D, TOK], BF16)
    d_wv = nc.dram_tensor("d_wv", [2, D, TOK], BF16)
    d_wo = nc.dram_tensor("d_wo", [2, D, D], BF16)
    d_wilr = nc.dram_tensor("d_wilr", [2, D, NH_OWN], BF16)
    cc_ln_in = nc.dram_tensor("cc_ln_in", [2 * D, TOK], BF16)
    cc_ln_out = nc.dram_tensor("cc_ln_out", [2 * 2 * D, TOK], BF16)
    d_xbT = nc.dram_tensor("d_xbT", [8, P, TSEQ], F32)
    d_xcT = nc.dram_tensor("d_xcT", [8, P, TSEQ], F32)
    d_xa = nc.dram_tensor("d_xa", [TSEQ, TOK], F32)
    d_co = nc.dram_tensor("d_co", [TSEQ, TOK], F32)
    d_cxb = nc.dram_tensor("d_cxb", [TSEQ, TOK], F32)
    d_z = nc.dram_tensor("d_z", [TSEQ, TOK], F32)
    cc_z_in = nc.dram_tensor("cc_z_in", [2 * TOK, TOK], BF16)
    cc_z_out = nc.dram_tensor("cc_z_out", [2 * 2 * TOK, TOK], BF16)
    d_zown = nc.dram_tensor("d_zown", [2 * TOK, TOK], BF16)
    d_zsel = nc.dram_tensor("d_zsel", [2 * TOK, TOK], BF16)

    def dap(tensor, offset, dims):
        return bass.AP(tensor, offset, dims)

    def sb_gap(t, ncols, col_off, col_dims):
        return bass.AP(t, col_off, [[32 * ncols, 4], [ncols, 16]] + col_dims)

    stack = contextlib.ExitStack()
    with stack:
        R = stack.enter_context(nc.sbuf_tensor("R", [P, DCH * TOK], F32))
        sg_sb = stack.enter_context(nc.sbuf_tensor("sg_sb", [P, JC], F32))
        su_sb = stack.enter_context(nc.sbuf_tensor("su_sb", [P, JC], F32))
        sd_sb = stack.enter_context(nc.sbuf_tensor("sd_sb", [P, DCH], F32))
        ones1_sb = stack.enter_context(nc.sbuf_tensor("ones1_sb", [1, P], F32))
        onescol_sb = stack.enter_context(nc.sbuf_tensor("onescol_sb", [P, 1], F32))
        allones_sb = stack.enter_context(nc.sbuf_tensor("allones_sb", [P, HF], F32))
        ident_sb = stack.enter_context(nc.sbuf_tensor("ident_sb", [P, P], F32))
        amask_sb = stack.enter_context(nc.sbuf_tensor("amask_sb", [P, 64], F32))
        tokmul_sb = stack.enter_context(nc.sbuf_tensor("tokmul_sb", [P, 1], F32))
        self_sb = stack.enter_context(nc.sbuf_tensor("self_sb", [P, 4], F32))
        bilr_sb = stack.enter_context(nc.sbuf_tensor("bilr_sb", [1, NH_OWN], F32))

        psum = stack.enter_context(nc.psum_tensor("psum", [P, 8 * 512], F32))
        block = stack.enter_context(nc.Block())

        C = {}
        for name in [
            "x_ld", "const_ld", "gu_ld", "wd_ld", "gu_mm", "ep", "ga",
            "fu", "fhi", "dn_mm", "dn_tmp", "dn_evac",
            "sq", "sqmm", "lnp", "bc_mm", "bc", "t2", "nhi", "nlo", "r2",
            "outst", "wsp_ld", "wsp_hi", "wsp_lo", "wsp_st",
            "agst", "ag", "pj_rhs", "pj_aw", "pj_mm", "pj_ev", "pj_st",
            "ilr_mm", "ilr_s", "co_t", "xr_ld", "tr_mm", "cxb_ev",
            "xa_rhs", "xa_mm", "xa_ev",
            "sc_ld", "sc_mm1", "sc_ev1", "sc_mm2", "sc_zb", "sc_w", "sc_init",
            "zst", "zt_ld", "zt_mm", "zt_ev", "zt_hl", "zt_st", "zcc",
            "wo_rhs", "wo_aw", "wo_mm", "wo_ev", "misc", "sc_w1",
        ]:
            C[name] = Ctr(stack, nc, "s_" + name)

        @block.sync
        def _(sync):
            for ic in range(DCH):
                i = sync.dma_start(out=R[:, ic * TOK:(ic + 1) * TOK],
                                   in_=xT[ic * P:(ic + 1) * P, :])
                C["x_ld"].inc(i, 16)
            for dst, src in ((sg_sb, sgate), (su_sb, sup), (sd_sb, sdown),
                             (ones1_sb, ones1), (onescol_sb, onescol),
                             (allones_sb, allones), (ident_sb, ident),
                             (amask_sb, amask), (tokmul_sb, tokmul),
                             (self_sb, selflags), (bilr_sb, bilr)):
                i = sync.dma_start(out=dst[:], in_=src)
                C["const_ld"].inc(i, 16)
        NCONST = 11

        def emit_weight_split(ws):
            jobs = []
            for (src, dst, w) in ((wqT, d_wq, TOK), (wkT, d_wk, TOK),
                                  (wvT, d_wv, TOK), (woT, d_wo, D),
                                  (wilrT, d_wilr, NH_OWN)):
                nw = (w + TOK - 1) // TOK
                for ic in range(DCH):
                    for wi in range(nw):
                        cw = min(TOK, w - wi * TOK)
                        jobs.append((src[ic * P:(ic + 1) * P, wi * TOK:wi * TOK + cw],
                                     dst, ic, wi, cw))
            import os
            _nws = int(os.environ.get("NWS", "0"))
            if _nws:
                jobs = jobs[:_nws]

            def store(scalar, bi):
                pj = jobs[bi]
                b = bi % 2
                scalar.wait_ge(C["wsp_lo"].h, bi + 1)
                i = scalar.dma_start(
                    out=pj[1].ap()[0, pj[2] * P:(pj[2] + 1) * P,
                                   pj[3] * TOK:pj[3] * TOK + pj[4]],
                    in_=ws.hi[:, b * TOK:b * TOK + pj[4]])
                C["wsp_st"].inc(i, 16)
                i = scalar.dma_start(
                    out=pj[1].ap()[1, pj[2] * P:(pj[2] + 1) * P,
                                   pj[3] * TOK:pj[3] * TOK + pj[4]],
                    in_=ws.lo[:, b * TOK:b * TOK + pj[4]])
                C["wsp_st"].inc(i, 16)

            @block.scalar
            def _(scalar):
                for bi, (srcap, dst, ic, wi, cw) in enumerate(jobs):
                    b = bi % 2
                    if bi >= 2:
                        store(scalar, bi - 2)
                    i = scalar.dma_start(out=ws.f32[:, b * TOK:b * TOK + cw], in_=srcap)
                    C["wsp_ld"].inc(i, 16)
                store(scalar, len(jobs) - 2)
                store(scalar, len(jobs) - 1)

            @block.vector
            def _(vector):
                for bi, (srcap, dst, ic, wi, cw) in enumerate(jobs):
                    b = bi % 2
                    vector.wait_ge(C["wsp_ld"].h, 16 * min(bi + 2, len(jobs)))
                    if bi >= 2:
                        vector.wait_ge(C["wsp_st"].h, 32 * bi)
                    i = vector.tensor_copy(ws.hi[:, b * TOK:b * TOK + cw],
                                           ws.f32[:, b * TOK:b * TOK + cw])
                    C["wsp_hi"].inc(i, 1)
                    vector.wait_ge(C["wsp_hi"].h, C["wsp_hi"].v)
                    i = vector.tensor_tensor(ws.lo[:, b * TOK:b * TOK + cw],
                                             ws.f32[:, b * TOK:b * TOK + cw],
                                             ws.hi[:, b * TOK:b * TOK + cw],
                                             ALU.subtract)
                    C["wsp_lo"].inc(i, 1)

        def emit_ln(f, tb_off):
            b_r2 = C["r2"].v
            b_woev = C["wo_ev"].v
            b_agst_ln = C["agst"].v
            b_sq = C["sq"].v
            b_sqmm = C["sqmm"].v
            b_lnp = C["lnp"].v
            b_bcmm = C["bc_mm"].v
            b_bc = C["bc"].v
            b_t2 = C["t2"].v
            b_nhi = C["nhi"].v
            b_dn_tmp = C["dn_tmp"].v
            ps_sum = psum[0:1, 4 * 512:4 * 512 + TB]
            ps_sq = psum[0:1, 5 * 512:5 * 512 + TB]

            @block.scalar
            def _(scalar):
                scalar.wait_ge(C["x_ld"].h, 16 * DCH)
                scalar.wait_ge(C["r2"].h, b_r2)
                scalar.wait_ge(C["wo_ev"].h, b_woev)
                for ic in range(DCH):
                    b = ic % 2
                    if ic >= 2:
                        scalar.wait_ge(C["sqmm"].h, b_sqmm + (ic - 1))
                    i = scalar.activation(f.sqt[:, b * TB:(b + 1) * TB],
                                          R[:, ic * TOK + tb_off:ic * TOK + tb_off + TB],
                                          AF.Square)
                    C["sq"].inc(i, 1)

            @block.tensor
            def _(tensor):
                tensor.wait_ge(C["x_ld"].h, 16 * DCH)
                tensor.wait_ge(C["const_ld"].h, 16 * NCONST)
                tensor.wait_ge(C["dn_tmp"].h, b_dn_tmp)
                tensor.wait_ge(C["r2"].h, b_r2)
                tensor.wait_ge(C["wo_ev"].h, b_woev)
                tensor.wait_ge(C["bc"].h, b_bc)
                tensor.wait_ge(C["lnp"].h, b_lnp)
                for ic in range(DCH):
                    tensor.matmul(ps_sum, onescol_sb[:, 0:1],
                                  R[:, ic * TOK + tb_off:ic * TOK + tb_off + TB],
                                  start=(ic == 0), stop=(ic == DCH - 1))
                for ic in range(DCH):
                    b = ic % 2
                    tensor.wait_ge(C["sq"].h, b_sq + ic + 1)
                    i = tensor.matmul(ps_sq, onescol_sb[:, 0:1],
                                      f.sqt[:, b * TB:(b + 1) * TB],
                                      start=(ic == 0), stop=(ic == DCH - 1))
                    C["sqmm"].inc(i, 1)

            @block.vector
            def _(vector):
                vector.wait_ge(C["sqmm"].h, b_sqmm + DCH)
                i = vector.tensor_scalar(f.rows[:, 0:TB], ps_sum, 1.0 / D, None, ALU.mult)
                C["lnp"].inc(i, 1)
                i = vector.tensor_scalar(f.rows[:, TB:2 * TB], ps_sq, 1.0 / D, None, ALU.mult)
                C["lnp"].inc(i, 1)

            @block.scalar
            def _(scalar):
                scalar.wait_ge(C["lnp"].h, b_lnp + 1)
                i = scalar.activation(f.rows[:, 2 * TB:3 * TB], f.rows[:, 0:TB], AF.Square)
                C["lnp"].inc(i, 1)

            @block.vector
            def _(vector):
                vector.wait_ge(C["lnp"].h, b_lnp + 3)
                i = vector.tensor_tensor(f.rows[:, 4 * TB:5 * TB], f.rows[:, TB:2 * TB],
                                         f.rows[:, 2 * TB:3 * TB], ALU.subtract)
                C["misc"].inc(i, 1)
                vector.wait_ge(C["misc"].h, C["misc"].v)
                i = vector.tensor_scalar(f.rows[:, TB:2 * TB], f.rows[:, 4 * TB:5 * TB],
                                         EPS, None, ALU.add)
                C["lnp"].inc(i, 1)

            @block.scalar
            def _(scalar):
                scalar.wait_ge(C["lnp"].h, b_lnp + 4)
                i = scalar.activation(f.rows[:, 3 * TB:4 * TB], f.rows[:, TB:2 * TB], AF.Sqrt)
                C["lnp"].inc(i, 1)

            @block.vector
            def _(vector):
                vector.wait_ge(C["lnp"].h, b_lnp + 5)
                i = vector.reciprocal(f.rows[:, 4 * TB:5 * TB], f.rows[:, 3 * TB:4 * TB])
                C["lnp"].inc(i, 1)
                vector.wait_ge(C["lnp"].h, b_lnp + 6)
                i = vector.tensor_tensor(f.rows[:, 5 * TB:6 * TB], f.rows[:, 0:TB],
                                         f.rows[:, 4 * TB:5 * TB], ALU.mult)
                C["lnp"].inc(i, 1)
                vector.wait_ge(C["lnp"].h, b_lnp + 7)
                i = vector.tensor_scalar(f.rows[:, 6 * TB:7 * TB], f.rows[:, 5 * TB:6 * TB],
                                         -1.0, None, ALU.mult)
                C["lnp"].inc(i, 1)

            @block.tensor
            def _(tensor):
                tensor.wait_ge(C["lnp"].h, b_lnp + 8)
                i = tensor.matmul(psum[:, 4 * 512:4 * 512 + TB], ones1_sb[:, :],
                                  f.rows[0:1, 4 * TB:5 * TB], start=True, stop=True)
                C["bc_mm"].inc(i, 1)
                i = tensor.matmul(psum[:, 5 * 512:5 * 512 + TB], ones1_sb[:, :],
                                  f.rows[0:1, 6 * TB:7 * TB], start=True, stop=True)
                C["bc_mm"].inc(i, 1)

            @block.vector
            def _(vector):
                vector.wait_ge(C["bc_mm"].h, b_bcmm + 2)
                i = vector.tensor_copy(f.ab[:], psum[:, 4 * 512:4 * 512 + TB])
                C["bc"].inc(i, 1)
                i = vector.tensor_copy(f.bb[:], psum[:, 5 * 512:5 * 512 + TB])
                C["bc"].inc(i, 1)

            @block.vector
            def _(vector):
                vector.wait_ge(C["bc"].h, b_bc + 2)
                vector.wait_ge(C["agst"].h, b_agst_ln)
                for ic in range(DCH):
                    b = ic % 2
                    i = vector.tensor_tensor(f.t1[:, b * TB:(b + 1) * TB],
                                             R[:, ic * TOK + tb_off:ic * TOK + tb_off + TB],
                                             f.ab[:], ALU.mult)
                    C["misc"].inc(i, 1)
                    vector.wait_ge(C["misc"].h, C["misc"].v)
                    i = vector.tensor_tensor(f.t1[:, b * TB:(b + 1) * TB],
                                             f.t1[:, b * TB:(b + 1) * TB],
                                             f.bb[:], ALU.add)
                    C["t2"].inc(i, 1)
                    vector.wait_ge(C["nhi"].h, b_nhi + ic + 1)
                    i = vector.tensor_tensor(f.lnt_lo[:, ic * TB:(ic + 1) * TB],
                                             f.t1[:, b * TB:(b + 1) * TB],
                                             f.lnt_hi[:, ic * TB:(ic + 1) * TB],
                                             ALU.subtract)
                    C["nlo"].inc(i, 1)

            @block.scalar
            def _(scalar):
                scalar.wait_ge(C["agst"].h, b_agst_ln)
                for ic in range(DCH):
                    b = ic % 2
                    scalar.wait_ge(C["t2"].h, b_t2 + ic + 1)
                    i = scalar.activation(f.lnt_hi[:, ic * TB:(ic + 1) * TB],
                                          f.t1[:, b * TB:(b + 1) * TB], AF.Copy)
                    C["nhi"].inc(i, 1)

            return C["nlo"].v

        def emit_ffn_tb(f, tb_off, ln_tgt):
            a_gu_ld = C["gu_ld"].v
            a_wd_ld = C["wd_ld"].v
            a_gu_mm = C["gu_mm"].v
            a_ep = C["ep"].v
            a_ga = C["ga"].v
            a_fu = C["fu"].v
            a_fhi = C["fhi"].v
            a_dn_mm = C["dn_mm"].v
            a_dn_tmp = C["dn_tmp"].v
            a_dn_evac = C["dn_evac"].v

            a_womm_f = C["wo_mm"].v
            a_ztst_f = C["zt_st"].v

            @block.gpsimd
            def _(gpsimd):
                gpsimd.wait_ge(C["wo_mm"].h, a_womm_f)
                gpsimd.wait_ge(C["zt_st"].h, a_ztst_f)
                for jb in range(NJB):
                    for jc8 in range(JBC):
                        jc = jb * JBC + jc8
                        b = jc % 2
                        if jc >= 2:
                            gpsimd.wait_ge(C["gu_mm"].h, a_gu_mm + jc - 1)
                        i = gpsimd.dma_start(
                            out=f.wg[:, b * DCH * P:(b + 1) * DCH * P],
                            in_=wg_r[jc].rearrange("p ic o -> p (ic o)"))
                        C["gu_ld"].inc(i, 16)
                        i = gpsimd.dma_start(
                            out=f.wu[:, b * DCH * P:(b + 1) * DCH * P],
                            in_=wu_r[jc].rearrange("p ic o -> p (ic o)"))
                        C["gu_ld"].inc(i, 16)
                    for o2t in range(DCH):
                        k = jb * DCH + o2t
                        b = k % 2
                        if k >= 2:
                            gpsimd.wait_ge(C["dn_mm"].h, a_dn_mm + (k - 1))
                        i = gpsimd.dma_start(
                            out=f.wd[:, b * JBC * P:(b + 1) * JBC * P],
                            in_=wd_r[jb, o2t].rearrange("p jc o -> p (jc o)"))
                        C["wd_ld"].inc(i, 16)

            @block.tensor
            def _(tensor):
                tensor.wait_ge(C["nlo"].h, ln_tgt)
                for jb in range(NJB):
                    for jc8 in range(JBC):
                        jc = jb * JBC + jc8
                        b = jc % 2
                        tensor.wait_ge(C["gu_ld"].h, a_gu_ld + 32 * min(jc + 2, (jb + 1) * JBC))
                        if jc >= 2:
                            tensor.wait_ge(C["ep"].h, a_ep + jc - 1)
                        for mi, wsb in ((0, f.wg), (1, f.wu)):
                            slot = b * 2 + mi
                            dst = psum[:, slot * 512:(slot + 1) * 512]
                            for ic in range(DCH):
                                for hl, lnt in ((0, f.lnt_hi), (1, f.lnt_lo)):
                                    i = tensor.matmul(
                                        dst,
                                        wsb[:, b * DCH * P + ic * P:b * DCH * P + (ic + 1) * P],
                                        lnt[:, ic * TB:(ic + 1) * TB],
                                        start=(ic == 0 and hl == 0),
                                        stop=(ic == DCH - 1 and hl == 1))
                        C["gu_mm"].inc(i, 1)
                    tensor.wait_ge(C["ep"].h, a_ep + (jb + 1) * JBC)
                    for o2t in range(DCH):
                        k = jb * DCH + o2t
                        kb = k % 2
                        slot = 4 + kb
                        tensor.wait_ge(C["wd_ld"].h, a_wd_ld + 16 * min(k + 2, (jb + 1) * DCH))
                        if k >= 2:
                            tensor.wait_ge(C["dn_evac"].h, a_dn_evac + (k - 1))
                        dst = psum[:, slot * 512:(slot + 1) * 512]
                        for jc8 in range(JBC):
                            for hl, fu in ((0, f.fuse_hi), (1, f.fuse_lo)):
                                i = tensor.matmul(
                                    dst,
                                    f.wd[:, kb * JBC * P + jc8 * P:kb * JBC * P + (jc8 + 1) * P],
                                    fu[:, jc8 * TB:(jc8 + 1) * TB],
                                    start=(jc8 == 0 and hl == 0),
                                    stop=(jc8 == JBC - 1 and hl == 1))
                        C["dn_mm"].inc(i, 1)

            @block.scalar
            def _(scalar):
                for jb in range(NJB):
                    for jc8 in range(JBC):
                        jc = jb * JBC + jc8
                        b = jc % 2
                        scalar.wait_ge(C["gu_mm"].h, a_gu_mm + jc + 1)
                        gslot = b * 2
                        i = scalar.activation(f.ga[:, b * TB:(b + 1) * TB],
                                              psum[:, gslot * 512:(gslot + 1) * 512],
                                              AF.Silu, scale=sg_sb[:, jc:jc + 1])
                        C["ga"].inc(i, 1)
                        scalar.wait_ge(C["fu"].h, a_fu + jc + 1)
                        if jb >= 1:
                            scalar.wait_ge(C["dn_mm"].h, a_dn_mm + DCH * jb)
                        i = scalar.activation(f.fuse_hi[:, jc8 * TB:(jc8 + 1) * TB],
                                              f.fu[:, b * TB:(b + 1) * TB], AF.Copy)
                        C["fhi"].inc(i, 1)
                    for o2t in range(DCH):
                        k = jb * DCH + o2t
                        kb = k % 2
                        slot = 4 + kb
                        scalar.wait_ge(C["dn_mm"].h, a_dn_mm + k + 1)
                        if k >= 2:
                            scalar.wait_ge(C["dn_evac"].h, a_dn_evac + (k - 1))
                        i = scalar.activation(f.dtmp[:, kb * TB:(kb + 1) * TB],
                                              psum[:, slot * 512:(slot + 1) * 512],
                                              AF.Copy, scale=sd_sb[:, o2t:o2t + 1])
                        C["dn_tmp"].inc(i, 1)

            @block.vector
            def _(vector):
                for jb in range(NJB):
                    for jc8 in range(JBC):
                        jc = jb * JBC + jc8
                        b = jc % 2
                        uslot = b * 2 + 1
                        vector.wait_ge(C["gu_mm"].h, a_gu_mm + jc + 1)
                        i = vector.tensor_scalar(f.ua[:, b * TB:(b + 1) * TB],
                                                 psum[:, uslot * 512:(uslot + 1) * 512],
                                                 su_sb[:, jc:jc + 1], None, ALU.mult)
                        C["misc"].inc(i, 1)
                        vector.wait_ge(C["misc"].h, C["misc"].v)
                        vector.wait_ge(C["ga"].h, a_ga + jc + 1)
                        i = vector.tensor_tensor(f.fu[:, b * TB:(b + 1) * TB],
                                                 f.ga[:, b * TB:(b + 1) * TB],
                                                 f.ua[:, b * TB:(b + 1) * TB], ALU.mult)
                        C["fu"].inc(i, 1)
                        vector.wait_ge(C["fhi"].h, a_fhi + jc + 1)
                        vector.wait_ge(C["fu"].h, C["fu"].v)
                        i = vector.tensor_tensor(f.fuse_lo[:, jc8 * TB:(jc8 + 1) * TB],
                                                 f.fu[:, b * TB:(b + 1) * TB],
                                                 f.fuse_hi[:, jc8 * TB:(jc8 + 1) * TB],
                                                 ALU.subtract)
                        C["ep"].inc(i, 1)
                    for o2t in range(DCH):
                        k = jb * DCH + o2t
                        kb = k % 2
                        vector.wait_ge(C["dn_tmp"].h, a_dn_tmp + k + 1)
                        i = vector.tensor_tensor(
                            R[:, o2t * TOK + tb_off:o2t * TOK + tb_off + TB],
                            R[:, o2t * TOK + tb_off:o2t * TOK + tb_off + TB],
                            f.dtmp[:, kb * TB:(kb + 1) * TB], ALU.add)
                        C["dn_evac"].inc(i, 1)

        def emit_rscale(factor, tb_off):
            b_dnev = C["dn_evac"].v
            b_t2r = C["t2"].v

            @block.vector
            def _(vector):
                vector.wait_ge(C["dn_evac"].h, b_dnev)
                vector.wait_ge(C["t2"].h, b_t2r)
                for ic in range(DCH):
                    i = vector.tensor_scalar(
                        R[:, ic * TOK + tb_off:ic * TOK + tb_off + TB],
                        R[:, ic * TOK + tb_off:ic * TOK + tb_off + TB],
                        factor, None, ALU.mult)
                    C["r2"].inc(i, 1)

        def alloc_ffn(st, sfx=""):
            f = NS()
            f.lnt_hi = st.enter_context(nc.sbuf_tensor("lnt_hi" + sfx, [P, DCH * TB], BF16))
            f.lnt_lo = st.enter_context(nc.sbuf_tensor("lnt_lo" + sfx, [P, DCH * TB], BF16))
            f.fuse_hi = st.enter_context(nc.sbuf_tensor("fuse_hi" + sfx, [P, JBC * TB], BF16))
            f.fuse_lo = st.enter_context(nc.sbuf_tensor("fuse_lo" + sfx, [P, JBC * TB], BF16))
            f.wg = st.enter_context(nc.sbuf_tensor("wg_sb" + sfx, [P, 2 * DCH * P], BF16))
            f.wu = st.enter_context(nc.sbuf_tensor("wu_sb" + sfx, [P, 2 * DCH * P], BF16))
            f.wd = st.enter_context(nc.sbuf_tensor("wd_sb" + sfx, [P, 2 * JBC * P], BF16))
            f.sqt = st.enter_context(nc.sbuf_tensor("sqt" + sfx, [P, 2 * TB], F32))
            f.rows = st.enter_context(nc.sbuf_tensor("rows" + sfx, [1, 7 * TB], F32))
            f.ab = st.enter_context(nc.sbuf_tensor("ab_sb" + sfx, [P, TB], F32))
            f.bb = st.enter_context(nc.sbuf_tensor("bb_sb" + sfx, [P, TB], F32))
            f.t1 = st.enter_context(nc.sbuf_tensor("t1buf" + sfx, [P, 2 * TB], F32))
            f.ga = st.enter_context(nc.sbuf_tensor("gabuf" + sfx, [P, 2 * TB], F32))
            f.ua = st.enter_context(nc.sbuf_tensor("uabuf" + sfx, [P, 2 * TB], F32))
            f.fu = st.enter_context(nc.sbuf_tensor("fubuf" + sfx, [P, 2 * TB], F32))
            f.dtmp = st.enter_context(nc.sbuf_tensor("dtmp" + sfx, [P, 2 * TB], F32))
            return f

        def emit_ffn_app(f):
            for tb in range(NTB):
                tb_off = tb * TB
                ln_tgt = emit_ln(f, tb_off)
                emit_rscale(2.0, tb_off)
                emit_ffn_tb(f, tb_off, ln_tgt)
                emit_rscale(0.5, tb_off)

        def emit_ln2_store(f, tb_off, ln_tgt):
            @block.sync
            def _(sync):
                sync.wait_ge(C["nlo"].h, ln_tgt)
                for ic in range(DCH):
                    i = sync.dma_start(
                        out=cc_ln_in.ap()[ic * P:(ic + 1) * P, tb_off:tb_off + TB],
                        in_=f.lnt_hi[:, ic * TB:(ic + 1) * TB])
                    C["agst"].inc(i, 16)
                    i = sync.dma_start(
                        out=cc_ln_in.ap()[D + ic * P:D + (ic + 1) * P, tb_off:tb_off + TB],
                        in_=f.lnt_lo[:, ic * TB:(ic + 1) * TB])
                    C["agst"].inc(i, 16)

        def emit_projections(a):
            ag_tgt = C["agst"].v
            ws_tgt = C["wsp_st"].v

            @block.gpsimd
            def _(gpsimd):
                gpsimd.wait_ge(C["agst"].h, ag_tgt)
                gpsimd.wait_ge(C["wsp_st"].h, ws_tgt)
                i = gpsimd.collective_compute(
                    "AllGather", ALU.bypass, replica_groups=PAIRS,
                    ins=[cc_ln_in.ap().opt()], outs=[cc_ln_out.ap().opt()])
                C["ag"].inc(i, 1)
                gpsimd.wait_ge(C["ag"].h, 1)
                for hl in range(2):
                    i = gpsimd.dma_start(
                        out=a.wilr[:, hl * DCH * NH_OWN:(hl + 1) * DCH * NH_OWN],
                        in_=dap(d_wilr, hl * D * NH_OWN,
                                [[NH_OWN, P], [P * NH_OWN, DCH], [1, NH_OWN]]))
                    C["pj_rhs"].inc(i, 16)

            b_rhs0 = C["pj_rhs"].v

            for pt in range(4):
                rr = pt // 2
                rc = (pt % 2) * TB
                b_rhs = C["pj_rhs"].v
                b_aw = C["pj_aw"].v
                b_mm = C["pj_mm"].v
                b_ev = C["pj_ev"].v
                b_st = C["pj_st"].v
                b_ilrmm = C["ilr_mm"].v
                b_ilrs = C["ilr_s"].v
                b_cot = C["co_t"].v
                b_xr = C["xr_ld"].v
                b_tr = C["tr_mm"].v
                b_cxb = C["cxb_ev"].v
                b_xarhs = C["xa_rhs"].v
                b_xamm = C["xa_mm"].v
                b_xaev = C["xa_ev"].v

                @block.gpsimd
                def _(gpsimd, pt=pt, rr=rr, rc=rc, b_mm=b_mm, b_xamm=b_xamm,
                      b_ilrmm=b_ilrmm, b_st_xb=b_st, b_tr=b_tr):
                    if pt >= 1:
                        gpsimd.wait_ge(C["pj_mm"].h, b_mm)
                        gpsimd.wait_ge(C["xa_mm"].h, b_xamm)
                        gpsimd.wait_ge(C["ilr_mm"].h, b_ilrmm)
                    for hl in range(2):
                        dst = a.rhs_hi if hl == 0 else a.rhs_lo
                        i = gpsimd.dma_start(
                            out=dst[:, :],
                            in_=dap(cc_ln_out,
                                    (rr * 2 * D + hl * D) * TOK + rc,
                                    [[TOK, P], [P * TOK, DCH], [1, TB]]))
                        C["pj_rhs"].inc(i, 16)
                    for mat, wt in ((0, d_wk), (1, d_wq)):
                        for ot in range(8):
                            j = mat * 8 + ot
                            if j >= 2:
                                gpsimd.wait_ge(C["pj_mm"].h, b_mm + j - 1)
                            for hl in range(2):
                                dst = a.aw_hi if hl == 0 else a.aw_lo
                                bslot = j % 2
                                i = gpsimd.dma_start(
                                    out=dst[:, bslot * DCH * P:(bslot + 1) * DCH * P],
                                    in_=dap(wt, hl * D * TOK + ot * P,
                                            [[TOK, P], [P * TOK, DCH], [1, P]]))
                                C["pj_aw"].inc(i, 16)
                    gpsimd.wait_ge(C["pj_st"].h, b_st_xb + 128)
                    for tt in range(4):
                        if tt >= 2:
                            gpsimd.wait_ge(C["tr_mm"].h, b_tr + (tt - 1) * 8)
                        i = gpsimd.dma_start(
                            out=a.xrt[:, (tt % 2) * 1024:(tt % 2) * 1024 + 1024],
                            in_=dap(d_xbT, pt * TB + tt * P,
                                    [[TSEQ, P], [P * TSEQ, 8], [1, P]]))
                        C["xr_ld"].inc(i, 16)
                    for fh in range(2):
                        for ic in range(DCH):
                            t = fh * DCH + ic
                            if t >= 2:
                                gpsimd.wait_ge(C["xa_mm"].h, b_xamm + t - 1)
                            for hl in range(2):
                                dst = a.xarhs_hi if hl == 0 else a.xarhs_lo
                                bslot = t % 2
                                i = gpsimd.dma_start(
                                    out=dst[:, bslot * TB:(bslot + 1) * TB],
                                    in_=dap(d_wv, hl * D * TOK + ic * P * TOK + fh * TB,
                                            [[TOK, P], [1, TB]]))
                                C["xa_rhs"].inc(i, 16)

                @block.tensor
                def _(tensor, pt=pt, b_rhs=b_rhs, b_aw=b_aw, b_ilrmm=b_ilrmm,
                      b_ilrs=b_ilrs, b_ev=b_ev, b_xr=b_xr, b_cxb=b_cxb,
                      b_xarhs=b_xarhs, b_xaev=b_xaev, b_mm=b_mm):
                    tensor.wait_ge(C["pj_rhs"].h, b_rhs + 32)
                    if pt == 0:
                        tensor.wait_ge(C["wsp_st"].h, C["wsp_st"].v)
                    tensor.wait_ge(C["ilr_s"].h, b_ilrs)
                    first = True
                    for (lh, wh) in ((0, 0), (1, 0), (0, 1)):
                        lnt = a.rhs_hi if lh == 0 else a.rhs_lo
                        for ic in range(DCH):
                            for tt in range(4):
                                ps = psum[0:P, 4 * 512 + tt * NH_OWN:4 * 512 + (tt + 1) * NH_OWN]
                                i = tensor.matmul(
                                    ps,
                                    lnt[:, ic * TB + tt * P:ic * TB + (tt + 1) * P],
                                    a.wilr[:, wh * DCH * NH_OWN + ic * NH_OWN:
                                           wh * DCH * NH_OWN + (ic + 1) * NH_OWN],
                                    start=first, stop=False)
                                first = False
                    for tt in range(4):
                        ps = psum[0:P, 4 * 512 + tt * NH_OWN:4 * 512 + (tt + 1) * NH_OWN]
                        i = tensor.matmul(ps, ones1_sb[:, :], bilr_sb[:, :],
                                          start=False, stop=(tt == 3))
                        C["ilr_mm"].inc(i, 1)
                    for mat in range(2):
                        for ot in range(8):
                            j = mat * 8 + ot
                            bslot = j % 2
                            pslot = j % 2
                            tensor.wait_ge(C["pj_aw"].h, b_aw + 32 * min(j + 2, 16))
                            if j >= 2:
                                tensor.wait_ge(C["pj_ev"].h, b_ev + j - 1)
                            else:
                                tensor.wait_ge(C["xa_ev"].h, b_xaev)
                            dst = psum[:, pslot * 512:(pslot + 1) * 512]
                            for (lh, wh) in ((0, 0), (1, 0), (0, 1)):
                                lnt = a.rhs_hi if lh == 0 else a.rhs_lo
                                aw = a.aw_hi if wh == 0 else a.aw_lo
                                for ic in range(DCH):
                                    i = tensor.matmul(
                                        dst,
                                        aw[:, bslot * DCH * P + ic * P:bslot * DCH * P + (ic + 1) * P],
                                        lnt[:, ic * TB:(ic + 1) * TB],
                                        start=(lh == 0 and wh == 0 and ic == 0),
                                        stop=(lh == 0 and wh == 1 and ic == DCH - 1))
                            C["pj_mm"].inc(i, 1)
                    for tt in range(4):
                        tensor.wait_ge(C["xr_ld"].h, b_xr + 16 * min(tt + 2, 4))
                        for ot in range(8):
                            n = tt * 8 + ot
                            half = n % 2
                            if n >= 2:
                                tensor.wait_ge(C["cxb_ev"].h, b_cxb + n - 1)
                            i = tensor.transpose(
                                psum[:, (5 + half) * 512:(5 + half) * 512 + P],
                                a.xrt[:, (tt % 2) * 1024 + ot * P:(tt % 2) * 1024 + (ot + 1) * P],
                                ident_sb[:])
                            C["tr_mm"].inc(i, 1)
                    for fh in range(2):
                        for ic in range(DCH):
                            t = fh * DCH + ic
                            bslot = t % 2
                            tensor.wait_ge(C["xa_rhs"].h, b_xarhs + 32 * min(t + 2, 2 * DCH))
                            if ic == 0:
                                tensor.wait_ge(C["xa_ev"].h, b_xaev + fh * 4)
                                if fh == 0:
                                    tensor.wait_ge(C["pj_ev"].h, b_ev + 16)
                            for tt in range(4):
                                ps = psum[:, tt * 512:(tt + 1) * 512]
                                for (lh, wh) in ((0, 0), (1, 0), (0, 1)):
                                    lnt = a.rhs_hi if lh == 0 else a.rhs_lo
                                    xr = a.xarhs_hi if wh == 0 else a.xarhs_lo
                                    i = tensor.matmul(
                                        ps,
                                        lnt[:, ic * TB + tt * P:ic * TB + (tt + 1) * P],
                                        xr[:, bslot * TB:(bslot + 1) * TB],
                                        start=(ic == 0 and lh == 0 and wh == 0),
                                        stop=(ic == DCH - 1 and lh == 0 and wh == 1))
                            C["xa_mm"].inc(i, 1)

                @block.scalar
                def _(scalar, b_ilrmm=b_ilrmm, b_cot=b_cot):
                    scalar.wait_ge(C["co_t"].h, b_cot)
                    for tt in range(4):
                        scalar.wait_ge(C["ilr_mm"].h, b_ilrmm + 4)
                        i = scalar.activation(
                            a.ilr[:, tt * NH_OWN:(tt + 1) * NH_OWN],
                            psum[0:P, 4 * 512 + tt * NH_OWN:4 * 512 + (tt + 1) * NH_OWN],
                            AF.Sigmoid)
                        C["ilr_s"].inc(i, 1)

                @block.vector
                def _(vector, pt=pt, b_ilrs=b_ilrs, b_mm=b_mm, b_tr=b_tr,
                      b_cot=b_cot, b_xamm=b_xamm, b_st=b_st, b_cxb=b_cxb,
                      b_ev=b_ev):
                    vector.wait_ge(C["ilr_s"].h, b_ilrs + 4)
                    i = vector.tensor_scalar(a.ilr[:, :], a.ilr[:, :],
                                             tokmul_sb[:, 0:1], None, ALU.mult)
                    C["ilr_s"].inc(i, 1)
                    for j in range(16):
                        pslot = j % 2
                        vector.wait_ge(C["pj_mm"].h, b_mm + j + 1)
                        if j >= 2:
                            vector.wait_ge(C["pj_st"].h, b_st + 16 * j)
                        i = vector.tensor_copy(a.pj[:, pslot * 512:(pslot + 1) * 512],
                                               psum[:, pslot * 512:(pslot + 1) * 512])
                        C["pj_ev"].inc(i, 1)
                    for tt in range(4):
                        cb = tt % 2
                        if tt >= 2:
                            vector.wait_ge(C["pj_st"].h, b_st + 16 * (16 + 2 * tt))
                        for hh in range(NH_OWN):
                            i = vector.tensor_scalar(
                                a.cot[:, cb * TOK + hh * HF:cb * TOK + (hh + 1) * HF],
                                allones_sb[:, :],
                                a.ilr[:, tt * NH_OWN + hh:tt * NH_OWN + hh + 1],
                                None, ALU.mult)
                        C["co_t"].inc(i, 1)
                        vector.wait_ge(C["co_t"].h, C["co_t"].v)
                        if tt >= 1:
                            vector.wait_ge(C["pj_st"].h, b_st + 16 * (17 + 2 * tt))
                        for ot in range(8):
                            n = tt * 8 + ot
                            half = n % 2
                            vector.wait_ge(C["tr_mm"].h, b_tr + n + 1)
                            i = vector.tensor_tensor(
                                a.cxb[:, ot * P:(ot + 1) * P],
                                psum[:, (5 + half) * 512:(5 + half) * 512 + P],
                                a.cot[:, cb * TOK + ot * P:cb * TOK + (ot + 1) * P],
                                ALU.mult)
                            C["cxb_ev"].inc(i, 1)
                    for fh in range(2):
                        if fh == 1:
                            vector.wait_ge(C["pj_st"].h, b_st + 16 * 28)
                        for tt in range(4):
                            vector.wait_ge(C["xa_mm"].h, b_xamm + (fh + 1) * DCH)
                            i = vector.tensor_copy(
                                a.xaev[:, tt * 512:(tt + 1) * 512],
                                psum[:, tt * 512:(tt + 1) * 512])
                            C["xa_ev"].inc(i, 1)

                @block.sync
                def _(sync, pt=pt, b_ev=b_ev, b_cot=b_cot, b_cxb=b_cxb,
                      b_xaev=b_xaev, b_tr=b_tr):
                    for j in range(8):
                        sync.wait_ge(C["pj_ev"].h, b_ev + j + 1)
                        i = sync.dma_start(
                            out=d_xbT.ap()[j, :, pt * TB:(pt + 1) * TB],
                            in_=a.pj[:, (j % 2) * 512:(j % 2 + 1) * 512])
                        C["pj_st"].inc(i, 16)
                    for j in range(8, 16):
                        sync.wait_ge(C["pj_ev"].h, b_ev + j + 1)
                        i = sync.dma_start(
                            out=d_xcT.ap()[j - 8, :, pt * TB:(pt + 1) * TB],
                            in_=a.pj[:, (j % 2) * 512:(j % 2 + 1) * 512])
                        C["pj_st"].inc(i, 16)
                    for tt in range(4):
                        cb = tt % 2
                        sync.wait_ge(C["co_t"].h, b_cot + tt + 1)
                        i = sync.dma_start(
                            out=d_co.ap()[pt * TB + tt * P:pt * TB + (tt + 1) * P, :],
                            in_=a.cot[:, cb * TOK:(cb + 1) * TOK])
                        C["pj_st"].inc(i, 16)
                        sync.wait_ge(C["cxb_ev"].h, b_cxb + (tt + 1) * 8)
                        i = sync.dma_start(
                            out=d_cxb.ap()[pt * TB + tt * P:pt * TB + (tt + 1) * P, :],
                            in_=a.cxb[:, :])
                        C["pj_st"].inc(i, 16)
                    for fh in range(2):
                        for tt in range(4):
                            sync.wait_ge(C["xa_ev"].h, b_xaev + fh * 4 + tt + 1)
                            i = sync.dma_start(
                                out=d_xa.ap()[pt * TB + tt * P:pt * TB + (tt + 1) * P,
                                              fh * TB:(fh + 1) * TB],
                                in_=a.xaev[:, tt * 512:(tt + 1) * 512])
                            C["pj_st"].inc(i, 16)

        def alloc_proj(st):
            a = NS()
            a.rhs_hi = st.enter_context(nc.sbuf_tensor("rhs_hi", [P, DCH * TB], BF16))
            a.rhs_lo = st.enter_context(nc.sbuf_tensor("rhs_lo", [P, DCH * TB], BF16))
            a.aw_hi = st.enter_context(nc.sbuf_tensor("aw_hi", [P, 2 * DCH * P], BF16))
            a.aw_lo = st.enter_context(nc.sbuf_tensor("aw_lo", [P, 2 * DCH * P], BF16))
            a.wilr = st.enter_context(nc.sbuf_tensor("wilr_sb", [P, 2 * DCH * NH_OWN], BF16))
            a.ilr = st.enter_context(nc.sbuf_tensor("ilr_sb", [P, 4 * NH_OWN], F32))
            a.cot = st.enter_context(nc.sbuf_tensor("cot", [P, 2 * TOK], F32))
            a.pj = st.enter_context(nc.sbuf_tensor("pjev", [P, 2 * 512], F32))
            a.xrt = st.enter_context(nc.sbuf_tensor("xrt", [P, 2 * 1024], F32))
            a.cxb = st.enter_context(nc.sbuf_tensor("cxbev", [P, 8 * P], F32))
            a.xarhs_hi = st.enter_context(nc.sbuf_tensor("xarhs_hi", [P, 2 * TB], BF16))
            a.xarhs_lo = st.enter_context(nc.sbuf_tensor("xarhs_lo", [P, 2 * TB], BF16))
            a.xaev = st.enter_context(nc.sbuf_tensor("xaev", [P, 4 * 512], F32))
            return a

        def emit_scan(s):
            import os
            NCH = int(os.environ.get("SCAN_CHUNKS", str(NCHUNKS)))
            NGR = int(os.environ.get("SCAN_GRPS", str(NGRP)))
            MSK = int(os.environ.get("SCAN_MM", "31"))
            SDVE = int(os.environ.get("SCAN_DVE", "1"))
            b_ld = C["sc_ld"].v
            b_mm1 = C["sc_mm1"].v
            b_ev1 = C["sc_ev1"].v
            b_mm2 = C["sc_mm2"].v
            b_zb = C["sc_zb"].v
            b_w = C["sc_w"].v
            b_zst = C["zst"].v
            pj_tgt = C["pj_st"].v
            PITCH = 2 * 2048

            @block.vector
            def _(vector):
                vector.wait_ge(C["pj_st"].h, pj_tgt)
                for t_ in (s.xa, s.co, s.cxb, s.zb, s.z1, s.asb, s.tmp):
                    i = vector.memset(t_[:, :], 0.0)
                    C["misc"].inc(i, 1)
            ms_tgt = C["misc"].v

            @block.gpsimd
            def _(gpsimd):
                gpsimd.wait_ge(C["pj_st"].h, pj_tgt)
                gpsimd.wait_ge(C["misc"].h, ms_tgt)
                i = gpsimd.dma_start(out=s.w1s[:, :], in_=w1pack)
                C["sc_w1"].inc(i, 16)
                for g in range(NGR):
                    gb = g % 2
                    if g >= 2:
                        gpsimd.wait_ge(C["sc_mm1"].h, b_mm1 + 8 * (g - 1))
                        gpsimd.wait_ge(C["sc_mm2"].h, b_mm2 + 8 * (g - 1))
                        gpsimd.wait_ge(C["sc_ev1"].h, b_ev1 + 16 * (g - 1))
                        gpsimd.wait_ge(C["sc_zb"].h, b_zb + 8 * (g - 1))
                    i = gpsimd.dma_start(
                        out=s.xbt[:, gb * 1024:(gb + 1) * 1024],
                        in_=dap(d_xbT, g * P, [[TSEQ, P], [P * TSEQ, 8], [1, P]]))
                    C["sc_ld"].inc(i, 16)
                    i = gpsimd.dma_start(
                        out=s.xct[:, gb * 1024:(gb + 1) * 1024],
                        in_=dap(d_xcT, g * P, [[TSEQ, P], [P * TSEQ, 8], [1, P]]))
                    C["sc_ld"].inc(i, 16)
                    for (dt_, dstt) in ((d_xa, s.xa), (d_co, s.co), (d_cxb, s.cxb)):
                        for G in range(4):
                            i = gpsimd.dma_start(
                                out=bass.AP(dstt, 32 * G * PITCH + gb * 2048,
                                            [[PITCH, 16], [1, 2048]]),
                                in_=dap(dt_, g * P * TOK + 256 * G,
                                        [[TOK, 16], [CS * TOK, 8], [1, 256]]))
                            C["sc_ld"].inc(i, 16)

            b_w1 = C["sc_w1"].v - 16

            def ld_tgt(k):
                return b_ld + 224 * min(k // 8 + 2, NGRP)

            @block.tensor
            def _(tensor):
                tensor.wait_ge(C["sc_w1"].h, b_w1 + 16)
                for k in range(NCH):
                    pi = k % 2
                    gb = (k // 8) % 2
                    kk = k % 8
                    tensor.wait_ge(C["sc_ld"].h, ld_tgt(k))
                    if SDVE:
                        tensor.wait_ge(C["sc_w"].h, b_w + k)
                    if k >= 2 and SDVE:
                        tensor.wait_ge(C["sc_zb"].h, b_zb + k - 1)
                        tensor.wait_ge(C["sc_ev1"].h, b_ev1 + 2 * k - 2)
                    for st_ in range(16):
                        q = st_ % 2
                        pr = st_ // 2
                        G = st_ // 4
                        wt = st_ % 4
                        xb_l = s.xbt[64 * q:64 * q + 64,
                                     gb * 1024 + pr * P + CS * kk:gb * 1024 + pr * P + CS * kk + CS]
                        xc_l = s.xct[64 * q:64 * q + 64,
                                     gb * 1024 + pr * P + CS * kk:gb * 1024 + pr * P + CS * kk + CS]
                        w_r = s.w1s[64 * q:64 * q + 64, pr * HF:(pr + 1) * HF]
                        i = tensor.matmul(
                            psum[32 * G:32 * G + CS,
                                 wt * 512 + pi * HF:wt * 512 + (pi + 1) * HF],
                            xb_l, w_r, start=True, stop=True, tile_position=(64 * q, 32 * G))
                        if MSK & 2:
                            i = tensor.matmul(
                                psum[32 * G:32 * G + CS,
                                     wt * 512 + 128 + pi * HF:wt * 512 + 128 + (pi + 1) * HF],
                                xc_l, w_r, start=True, stop=True, tile_position=(64 * q, 32 * G))
                        if MSK & 4:
                            i = tensor.matmul(
                                psum[32 * G:32 * G + CS,
                                     wt * 512 + 256 + pi * CS:wt * 512 + 256 + (pi + 1) * CS],
                                xb_l, xc_l, start=True, stop=True, tile_position=(64 * q, 32 * G))
                    C["sc_mm1"].inc(i, 1)
                    if SDVE:
                        tensor.wait_ge(C["sc_ev1"].h, b_ev1 + 2 * k + 2)
                    for st_ in range(16):
                        q = st_ % 2
                        pr = st_ // 2
                        G = st_ // 4
                        wt = st_ % 4
                        a_l = s.asb[32 * G:32 * G + CS, pi * 64 + wt * CS:pi * 64 + (wt + 1) * CS]
                        z_r = s.z1[32 * G:32 * G + CS, pi * 256 + wt * HF:pi * 256 + (wt + 1) * HF]
                        if MSK & 8:
                            i = tensor.matmul(
                                psum[32 * G:32 * G + CS,
                                     wt * 512 + 320 + pi * HF:wt * 512 + 320 + (pi + 1) * HF],
                                a_l, z_r, start=True, stop=True, tile_position=(32 * G, 32 * G))
                        cx_l = s.cxb[32 * G:32 * G + CS,
                                     gb * 2048 + kk * 256 + wt * HF:gb * 2048 + kk * 256 + (wt + 1) * HF]
                        if MSK & 16:
                            i = tensor.matmul(
                                psum[64 * q:64 * q + HF, 6 * 512 + pr * HF:6 * 512 + (pr + 1) * HF],
                                cx_l, z_r, start=True, stop=True,
                                tile_position=(32 * G, 64 * q))
                    if not (MSK & 24):
                        i = tensor.matmul(psum[0:CS, 7 * 512 + 256:7 * 512 + 256 + CS],
                                          s.xbt[0:64, 0:CS], s.xct[0:64, 0:CS],
                                          start=True, stop=True)
                    C["sc_mm2"].inc(i, 1)

            @block.vector
            def _(vector):
                vector.wait_ge(C["sc_w1"].h, C["sc_w1"].v)
                for k in range(NCH if SDVE else 0):
                    pi = k % 2
                    gb = (k // 8) % 2
                    kk = k % 8
                    g = k // 8
                    vector.wait_ge(C["sc_mm1"].h, b_mm1 + k + 1)
                    if k >= 2:
                        vector.wait_ge(C["sc_mm2"].h, b_mm2 + k - 1)
                    i = vector.tensor_tensor(
                        bass.AP(s.z1, pi * 256, [[512, P], [HF, 4], [1, HF]]),
                        bass.AP(psum, pi * HF, [[4096, P], [512, 4], [1, HF]]),
                        bass.AP(s.xa, gb * 2048 + kk * 256,
                                [[2 * 2048, P], [HF, 4], [1, HF]]),
                        ALU.subtract)
                    C["sc_ev1"].inc(i, 1)
                    i = vector.tensor_tensor(
                        bass.AP(s.asb, pi * 64, [[128, P], [CS, 4], [1, CS]]),
                        bass.AP(psum, 256 + pi * CS, [[4096, P], [512, 4], [1, CS]]),
                        bass.AP(amask_sb, 0, [[64, P], [CS, 4], [1, CS]]),
                        ALU.mult)
                    C["sc_ev1"].inc(i, 1)
                    vector.wait_ge(C["sc_mm2"].h, b_mm2 + k + 1)
                    if kk == 0 and g >= 2:
                        vector.wait_ge(C["zst"].h, b_zst + 64 * g)
                    i = vector.tensor_tensor(
                        bass.AP(s.tmp, pi * 256, [[512, P], [HF, 4], [1, HF]]),
                        bass.AP(psum, 320 + pi * HF, [[4096, P], [512, 4], [1, HF]]),
                        bass.AP(s.co, gb * 2048 + kk * 256,
                                [[2 * 2048, P], [HF, 4], [1, HF]]),
                        ALU.mult)
                    C["misc"].inc(i, 1)
                    vector.wait_ge(C["misc"].h, C["misc"].v)
                    i = vector.tensor_tensor(
                        bass.AP(s.zb, gb * 2048 + kk * 256,
                                [[2 * 2048, P], [HF, 4], [1, HF]]),
                        bass.AP(psum, 128 + pi * HF, [[4096, P], [512, 4], [1, HF]]),
                        bass.AP(s.tmp, pi * 256, [[512, P], [HF, 4], [1, HF]]), ALU.add)
                    C["sc_zb"].inc(i, 1)
                    i = vector.tensor_tensor(s.w1s[:, :], s.w1s[:, :],
                                             psum[:, 6 * 512:7 * 512], ALU.add)
                    C["sc_w"].inc(i, 1)

            @block.scalar
            def _(scalar):
                PITCH = 2 * 2048
                for g in range(max(1, NCH // 8) if SDVE else 0):
                    gb = g % 2
                    scalar.wait_ge(C["sc_zb"].h, b_zb + min(8 * (g + 1), NCH))
                    for G in range(4):
                        i = scalar.dma_start(
                            out=dap(d_z, g * P * TOK + 256 * G,
                                    [[TOK, 16], [CS * TOK, 8], [1, 256]]),
                            in_=bass.AP(s.zb, 32 * G * PITCH + gb * 2048,
                                        [[PITCH, 16], [256, 8], [1, 256]]))
                        C["zst"].inc(i, 16)

        def alloc_scan(st):
            s = NS()
            s.xbt = st.enter_context(nc.sbuf_tensor("sc_xbt", [P, 2 * 1024], F32))
            s.xct = st.enter_context(nc.sbuf_tensor("sc_xct", [P, 2 * 1024], F32))
            s.xa = st.enter_context(nc.sbuf_tensor("sc_xa", [P, 2 * 2048], F32))
            s.co = st.enter_context(nc.sbuf_tensor("sc_co", [P, 2 * 2048], F32))
            s.cxb = st.enter_context(nc.sbuf_tensor("sc_cxb", [P, 2 * 2048], F32))
            s.zb = st.enter_context(nc.sbuf_tensor("sc_zb", [P, 2 * 2048], F32))
            s.xa_t, s.co_t, s.cxb_t, s.zb_t = s.xa, s.co, s.cxb, s.zb
            s.z1 = st.enter_context(nc.sbuf_tensor("sc_z1", [P, 2 * 256], F32))
            s.asb = st.enter_context(nc.sbuf_tensor("sc_a", [P, 2 * 64], F32))
            s.tmp = st.enter_context(nc.sbuf_tensor("sc_tmp", [P, 2 * 256], F32))
            s.w1s = st.enter_context(nc.sbuf_tensor("sc_w1s", [P, 512], F32))
            s.w1 = st.enter_context(nc.sbuf_tensor("sc_w1", [P, 512], F32))
            return s

        def emit_zwo(zz):
            b_zld = C["zt_ld"].v
            b_zmm = C["zt_mm"].v
            b_zev = C["zt_ev"].v
            b_zhl = C["zt_hl"].v
            b_zst2 = C["zt_st"].v
            b_zcc = C["zcc"].v
            b_womm = C["wo_mm"].v
            b_woev = C["wo_ev"].v
            b_worhs = C["wo_rhs"].v
            b_woaw = C["wo_aw"].v
            zst_tgt = C["zst"].v

            @block.sync
            def _(sync):
                sync.wait_ge(C["zst"].h, zst_tgt)
                for ft in range(8):
                    fb = ft % 2
                    if ft >= 2:
                        sync.wait_ge(C["zt_mm"].h, b_zmm + 16 * (ft - 1))
                    i = sync.dma_start(
                        out=zz.zl[:, fb * 2048:(fb + 1) * 2048],
                        in_=dap(d_z, ft * P, [[TOK, P], [P * TOK, DCH], [1, P]]))
                    C["zt_ld"].inc(i, 16)

            @block.tensor
            def _(tensor):
                for ft in range(8):
                    fb = ft % 2
                    tensor.wait_ge(C["zt_ld"].h, b_zld + 16 * min(ft + 2, 8))
                    for t2 in range(DCH):
                        n = ft * DCH + t2
                        half = n % 2
                        if n >= 2:
                            tensor.wait_ge(C["zt_ev"].h, b_zev + n - 1)
                        i = tensor.transpose(
                            psum[:, (6 + half) * 512:(6 + half) * 512 + P],
                            zz.zl[:, fb * 2048 + t2 * P:fb * 2048 + (t2 + 1) * P],
                            ident_sb[:])
                        C["zt_mm"].inc(i, 1)

            @block.vector
            def _(vector):
                for ft in range(8):
                    for t2 in range(DCH):
                        n = ft * DCH + t2
                        half = n % 2
                        vector.wait_ge(C["zt_mm"].h, b_zmm + n + 1)
                        i = vector.tensor_copy(
                            zz.zt[:, t2 * P:(t2 + 1) * P],
                            psum[:, (6 + half) * 512:(6 + half) * 512 + P])
                        C["zt_ev"].inc(i, 1)
                    if ft >= 1:
                        vector.wait_ge(C["zt_st"].h, b_zst2 + 64 * ft)
                    vector.wait_ge(C["zt_ev"].h, C["zt_ev"].v - (8 - ft - 1) * DCH
                                   if False else b_zev + (ft + 1) * DCH)
                    i = vector.tensor_copy(zz.zhi[:, :], zz.zt[:, :])
                    C["misc"].inc(i, 1)
                    vector.wait_ge(C["misc"].h, C["misc"].v)
                    i = vector.tensor_tensor(zz.zlo[:, :], zz.zt[:, :], zz.zhi[:, :],
                                             ALU.subtract)
                    C["misc"].inc(i, 1)
                    vector.wait_ge(C["misc"].h, C["misc"].v)
                    for hl, zsrc in ((0, zz.zhi), (1, zz.zlo)):
                        i = vector.tensor_scalar(zz.t0[:, :], zsrc[:, 0:TOK],
                                                 self_sb[:, 0:1], None, ALU.mult)
                        i = vector.tensor_scalar(zz.t1[:, :], zsrc[:, TOK:2 * TOK],
                                                 self_sb[:, 1:2], None, ALU.mult)
                        C["misc"].inc(i, 1)
                        vector.wait_ge(C["misc"].h, C["misc"].v)
                        i = vector.tensor_tensor(zz.zown[:, hl * TOK:(hl + 1) * TOK],
                                                 zz.t0[:, :], zz.t1[:, :], ALU.add)
                        i = vector.tensor_scalar(zz.t2[:, :], zsrc[:, 0:TOK],
                                                 self_sb[:, 1:2], None, ALU.mult)
                        i = vector.tensor_scalar(zz.t3[:, :], zsrc[:, TOK:2 * TOK],
                                                 self_sb[:, 0:1], None, ALU.mult)
                        C["misc"].inc(i, 1)
                        vector.wait_ge(C["misc"].h, C["misc"].v)
                        i = vector.tensor_tensor(zz.zpart[:, hl * TOK:(hl + 1) * TOK],
                                                 zz.t2[:, :], zz.t3[:, :], ALU.add)
                    C["zt_hl"].inc(i, 1)

            @block.scalar
            def _(scalar):
                for ft in range(8):
                    scalar.wait_ge(C["zt_hl"].h, b_zhl + ft + 1)
                    for hl in range(2):
                        i = scalar.dma_start(
                            out=d_zown.ap()[hl * TOK + ft * P:hl * TOK + (ft + 1) * P, :],
                            in_=zz.zown[:, hl * TOK:(hl + 1) * TOK])
                        C["zt_st"].inc(i, 16)
                        i = scalar.dma_start(
                            out=cc_z_in.ap()[hl * TOK + ft * P:hl * TOK + (ft + 1) * P, :],
                            in_=zz.zpart[:, hl * TOK:(hl + 1) * TOK])
                        C["zt_st"].inc(i, 16)

            zt_st_tgt = C["zt_st"].v

            @block.gpsimd
            def _(gpsimd):
                gpsimd.wait_ge(C["zt_st"].h, zt_st_tgt)
                i = gpsimd.collective_compute(
                    "AllGather", ALU.bypass, replica_groups=PAIRS,
                    ins=[cc_z_in.ap().opt()], outs=[cc_z_out.ap().opt()])
                C["zcc"].inc(i, 1)
                gpsimd.wait_ge(C["zcc"].h, b_zcc + 1)
                for j in range(16):
                    jb = j % 2
                    if j >= 2:
                        gpsimd.wait_ge(C["zcc"].h, b_zcc + 1 + j - 1)
                    i = gpsimd.dma_start(
                        out=zz.zsl[:, jb * 2048:jb * 2048 + 1024],
                        in_=cc_z_out.ap()[j * P:(j + 1) * P, :])
                    C["wo_rhs"].inc(i, 16)
                    i = gpsimd.dma_start(
                        out=zz.zsl[:, jb * 2048 + 1024:jb * 2048 + 2048],
                        in_=cc_z_out.ap()[2 * TOK + j * P:2 * TOK + (j + 1) * P, :])
                    C["wo_rhs"].inc(i, 16)

            @block.vector
            def _(vector):
                b0 = C["wo_rhs"].v - 32 * 16
                for j in range(16):
                    jb = j % 2
                    vector.wait_ge(C["wo_rhs"].h, b0 + 32 * min(j + 2, 16))
                    if j >= 2:
                        vector.wait_ge(C["zt_st"].h, zt_st_tgt + 16 * j)
                    i = vector.tensor_scalar(zz.t0[:, :], zz.zsl[:, jb * 2048:jb * 2048 + 1024],
                                             self_sb[:, 2:3], None, ALU.mult)
                    i = vector.tensor_scalar(zz.t1[:, :],
                                             zz.zsl[:, jb * 2048 + 1024:jb * 2048 + 2048],
                                             self_sb[:, 3:4], None, ALU.mult)
                    C["misc"].inc(i, 1)
                    vector.wait_ge(C["misc"].h, C["misc"].v)
                    i = vector.tensor_tensor(zz.zsel[:, jb * TOK:(jb + 1) * TOK],
                                             zz.t0[:, :], zz.t1[:, :], ALU.add)
                    C["zcc"].inc(i, 1)

            @block.scalar
            def _(scalar):
                b0 = C["zcc"].v - 16
                for j in range(16):
                    jb = j % 2
                    scalar.wait_ge(C["zcc"].h, b0 + j + 1)
                    i = scalar.dma_start(out=d_zsel.ap()[j * P:(j + 1) * P, :],
                                         in_=zz.zsel[:, jb * TOK:(jb + 1) * TOK])
                    C["zt_st"].inc(i, 16)

            sel_done = C["zt_st"].v

            @block.gpsimd
            def _(gpsimd):
                gpsimd.wait_ge(C["zt_st"].h, sel_done)
                for th in range(2):
                    if th >= 1:
                        gpsimd.wait_ge(C["wo_mm"].h, b_womm + DCH)
                    for hl in range(2):
                        dst = zz.worhs_hi if hl == 0 else zz.worhs_lo
                        for (half, dz) in ((0, d_zown), (1, d_zsel)):
                            i = gpsimd.dma_start(
                                out=dst[:, half * 8 * TB:(half + 1) * 8 * TB],
                                in_=dap(dz, hl * TOK * TOK + th * TB,
                                        [[TOK, P], [P * TOK, 8], [1, TB]]))
                            C["wo_rhs"].inc(i, 16)

            b_worhs2 = C["wo_rhs"].v - 2 * 64

            @block.scalar
            def _(scalar):
                for th in range(2):
                    for ot in range(DCH):
                        j = th * DCH + ot
                        if j >= 2:
                            scalar.wait_ge(C["wo_mm"].h, b_womm + j - 1)
                        for hl in range(2):
                            dst = zz.woaw_hi if hl == 0 else zz.woaw_lo
                            bslot = j % 2
                            i = scalar.dma_start(
                                out=dst[:, bslot * DCH * P:(bslot + 1) * DCH * P],
                                in_=dap(d_wo, hl * D * D + ot * P,
                                        [[D, P], [P * D, DCH], [1, P]]))
                            C["wo_aw"].inc(i, 16)

            b_woaw2 = C["wo_aw"].v - 2 * DCH * 32

            @block.tensor
            def _(tensor):
                for th in range(2):
                    tensor.wait_ge(C["wo_rhs"].h, b_worhs2 + 64 * (th + 1))
                    for ot in range(DCH):
                        j = th * DCH + ot
                        bslot = j % 2
                        pslot = j % 2
                        tensor.wait_ge(C["wo_aw"].h, b_woaw2 + 32 * min(j + 2, 2 * DCH))
                        if j >= 2:
                            tensor.wait_ge(C["wo_ev"].h, b_woev + j - 1)
                        dst = psum[:, pslot * 512:(pslot + 1) * 512]
                        for (lh, wh) in ((0, 0), (1, 0), (0, 1)):
                            rh = zz.worhs_hi if lh == 0 else zz.worhs_lo
                            aw = zz.woaw_hi if wh == 0 else zz.woaw_lo
                            for ic in range(DCH):
                                i = tensor.matmul(
                                    dst,
                                    aw[:, bslot * DCH * P + ic * P:bslot * DCH * P + (ic + 1) * P],
                                    rh[:, ic * TB:(ic + 1) * TB],
                                    start=(lh == 0 and wh == 0 and ic == 0),
                                    stop=(lh == 0 and wh == 1 and ic == DCH - 1))
                        C["wo_mm"].inc(i, 1)

            @block.vector
            def _(vector):
                for th in range(2):
                    for ot in range(DCH):
                        j = th * DCH + ot
                        pslot = j % 2
                        vector.wait_ge(C["wo_mm"].h, b_womm + j + 1)
                        i = vector.tensor_tensor(
                            R[:, ot * TOK + th * TB:ot * TOK + (th + 1) * TB],
                            R[:, ot * TOK + th * TB:ot * TOK + (th + 1) * TB],
                            psum[:, pslot * 512:(pslot + 1) * 512], ALU.add)
                        C["wo_ev"].inc(i, 1)

        def alloc_zwo(st):
            zz = NS()
            zz.zl = st.enter_context(nc.sbuf_tensor("zt_zl", [P, 2 * 2048], F32))
            zz.zt = st.enter_context(nc.sbuf_tensor("zt_zt", [P, 2048], F32))
            zz.zhi = st.enter_context(nc.sbuf_tensor("zt_hi", [P, 2048], BF16))
            zz.zlo = st.enter_context(nc.sbuf_tensor("zt_lo", [P, 2048], BF16))
            zz.t0 = st.enter_context(nc.sbuf_tensor("zt_t0", [P, TOK], BF16))
            zz.t1 = st.enter_context(nc.sbuf_tensor("zt_t1", [P, TOK], BF16))
            zz.t2 = st.enter_context(nc.sbuf_tensor("zt_t2", [P, TOK], BF16))
            zz.t3 = st.enter_context(nc.sbuf_tensor("zt_t3", [P, TOK], BF16))
            zz.zown = st.enter_context(nc.sbuf_tensor("zt_zown", [P, 2 * TOK], BF16))
            zz.zpart = st.enter_context(nc.sbuf_tensor("zt_zpart", [P, 2 * TOK], BF16))
            zz.zsl = st.enter_context(nc.sbuf_tensor("zt_zsl", [P, 2 * 2048], BF16))
            zz.zsel = st.enter_context(nc.sbuf_tensor("zt_zsel", [P, 2 * TOK], BF16))
            zz.worhs_hi = st.enter_context(nc.sbuf_tensor("worhs_hi", [P, DCH * TB], BF16))
            zz.worhs_lo = st.enter_context(nc.sbuf_tensor("worhs_lo", [P, DCH * TB], BF16))
            zz.woaw_hi = st.enter_context(nc.sbuf_tensor("woaw_hi", [P, 2 * DCH * P], BF16))
            zz.woaw_lo = st.enter_context(nc.sbuf_tensor("woaw_lo", [P, 2 * DCH * P], BF16))
            return zz

        ffn1_stack = contextlib.ExitStack()
        f1 = alloc_ffn(ffn1_stack, "1")
        ws_stack = contextlib.ExitStack()
        WS_ON = (STAGE >= 10) or (2 <= STAGE <= 5)
        LN2_ON = (STAGE >= 12) or (2 <= STAGE <= 5)
        if WS_ON:
            ws = NS()
            ws.f32 = ws_stack.enter_context(nc.sbuf_tensor("wsp_f32", [P, 2 * TOK], F32))
            ws.hi = ws_stack.enter_context(nc.sbuf_tensor("wsp_hi", [P, 2 * TOK], BF16))
            ws.lo = ws_stack.enter_context(nc.sbuf_tensor("wsp_lo", [P, 2 * TOK], BF16))
            emit_weight_split(ws)
        if STAGE != 10:
            emit_ffn_app(f1)
        if LN2_ON:
            for tb in range(NTB):
                tgt = emit_ln(f1, tb * TB)
                emit_ln2_store(f1, tb * TB, tgt)
        if STAGE == 12:
            ag_tgt0 = C["agst"].v
            ws_tgt0 = C["wsp_st"].v

            @block.gpsimd
            def _(gpsimd):
                gpsimd.wait_ge(C["agst"].h, ag_tgt0)
                gpsimd.wait_ge(C["wsp_st"].h, ws_tgt0)
                i = gpsimd.collective_compute(
                    "AllGather", ALU.bypass, replica_groups=PAIRS,
                    ins=[cc_ln_in.ap().opt()], outs=[cc_ln_out.ap().opt()])
                C["ag"].inc(i, 1)
                gpsimd.wait_ge(C["ag"].h, 1)
        ws_stack.close()
        ffn1_stack.close()

        if STAGE >= 2:
            proj_stack = contextlib.ExitStack()
            a = alloc_proj(proj_stack)
            emit_projections(a)
            proj_stack.close()
        if STAGE >= 3:
            scan_stack = contextlib.ExitStack()
            s = alloc_scan(scan_stack)
            emit_scan(s)
            scan_stack.close()
        if STAGE >= 4:
            zwo_stack = contextlib.ExitStack()
            zz = alloc_zwo(zwo_stack)
            emit_zwo(zz)
            zwo_stack.close()
        if STAGE >= 5:
            ffn2_stack = contextlib.ExitStack()
            f2 = alloc_ffn(ffn2_stack, "2")
            emit_ffn_app(f2)
            ffn2_stack.close()

        @block.sync
        def _(sync):
            sync.wait_ge(C["r2"].h, C["r2"].v)
            if STAGE == 10:
                sync.wait_ge(C["wsp_st"].h, C["wsp_st"].v)
            if STAGE == 4:
                sync.wait_ge(C["wo_ev"].h, C["wo_ev"].v)
            for ic in range(DCH):
                i = sync.dma_start(out=out[ic * P:(ic + 1) * P, :],
                                   in_=R[:, ic * TOK:(ic + 1) * TOK])
                C["outst"].inc(i, 16)
            sync.wait_ge(C["outst"].h, C["outst"].v)

    return nc


def _prep_core_inputs(c, x, WqT, WkT, WvT, WoT, WilrT, b_ilr, W1,
                      wg_r, wu_r, wd_r, s_gate, s_up, s_down, consts):
    b, h = c // 2, c % 2
    d = {}
    d["xT"] = np.ascontiguousarray(x[b, 1024 * h:1024 * (h + 1), :].T)
    d["wqT"] = np.ascontiguousarray(WqT[:, 1024 * h:1024 * (h + 1)])
    d["wkT"] = np.ascontiguousarray(WkT[:, 1024 * h:1024 * (h + 1)])
    d["wvT"] = np.ascontiguousarray(WvT[:, 1024 * h:1024 * (h + 1)])
    d["woT"] = np.ascontiguousarray(
        np.concatenate([WoT[1024 * h:1024 * (h + 1)], WoT[1024 * (1 - h):1024 * (2 - h)]],
                       axis=0))
    sf = np.zeros((128, 4), np.float32)
    sf[:, 0] = 1.0 - h
    sf[:, 1] = h
    sf[:, 2] = h
    sf[:, 3] = 1.0 - h
    d["selflags"] = sf
    d["wilrT"] = np.ascontiguousarray(WilrT[:, 16 * h:16 * (h + 1)])
    d["bilr"] = np.ascontiguousarray(b_ilr[16 * h:16 * (h + 1)].reshape(1, 16))
    w1 = W1[16 * h:16 * (h + 1)]
    w1p = np.zeros((128, 8 * 64), dtype=np.float32)
    for p in range(8):
        w1p[0:64, 64 * p:64 * (p + 1)] = w1[2 * p]
        w1p[64:128, 64 * p:64 * (p + 1)] = w1[2 * p + 1]
    d["w1pack"] = w1p
    d["wg_r"] = wg_r
    d["wu_r"] = wu_r
    d["wd_r"] = wd_r
    d["sgate"] = s_gate
    d["sup"] = s_up
    d["sdown"] = s_down
    d.update(consts)
    return d


def kernel(L0_attn_hidden_states, L0_attn_Wq, L0_attn_Wk, L0_attn_Wv,
           L0_attn_Wo, L0_attn_W_ilr, L0_attn_b_ilr, L0_attn_W1,
           L0_ffn_w_gate, L0_ffn_w_up, L0_ffn_w_down,
           L0_ffn_s_gate, L0_ffn_s_up, L0_ffn_s_down):
    x = np.asarray(L0_attn_hidden_states, dtype=np.float32)
    WqT = np.ascontiguousarray(np.asarray(L0_attn_Wq, np.float32).T)
    WkT = np.ascontiguousarray(np.asarray(L0_attn_Wk, np.float32).T)
    WvT = np.ascontiguousarray(np.asarray(L0_attn_Wv, np.float32).T)
    WoT = np.ascontiguousarray(np.asarray(L0_attn_Wo, np.float32).T)
    WilrT = np.ascontiguousarray(np.asarray(L0_attn_W_ilr, np.float32).T)
    b_ilr = np.asarray(L0_attn_b_ilr, np.float32)
    W1 = np.asarray(L0_attn_W1, np.float32)

    wg = np.asarray(L0_ffn_w_gate)
    wu = np.asarray(L0_ffn_w_up)
    wd = np.asarray(L0_ffn_w_down)
    wg_r = np.ascontiguousarray(wg.reshape(DCH, P, JC, P).transpose(2, 1, 0, 3))
    wu_r = np.ascontiguousarray(wu.reshape(DCH, P, JC, P).transpose(2, 1, 0, 3))
    wd_r = np.ascontiguousarray(
        wd.reshape(NJB, JBC, P, DCH, P).transpose(0, 3, 2, 1, 4))
    s_gate = np.ascontiguousarray(np.asarray(L0_ffn_s_gate, np.float32).reshape(JC, P).T)
    s_up = np.ascontiguousarray(np.asarray(L0_ffn_s_up, np.float32).reshape(JC, P).T)
    s_down = np.ascontiguousarray(np.asarray(L0_ffn_s_down, np.float32).reshape(DCH, P).T)

    consts = {
        "ones1": np.ones((1, P), np.float32),
        "onescol": np.ones((P, 1), np.float32),
        "allones": np.ones((P, HF), np.float32),
        "ident": np.eye(P, dtype=np.float32),
        "amask": np.fromfunction(
            lambda p, q: ((p % 32) <= (q % 16)).astype(np.float32), (P, 64)),
        "tokmul": (-1.0 / (np.arange(P) % CS + 1) / HF).astype(np.float32).reshape(P, 1),
    }

    nc = build_program()
    in_maps = [
        _prep_core_inputs(c, x, WqT, WkT, WvT, WoT, WilrT, b_ilr, W1,
                          wg_r, wu_r, wd_r, s_gate, s_up, s_down, consts)
        for c in range(8)
    ]
    res = run_bass_kernel_spmd(nc, in_maps, core_ids=list(range(8)))
    outp = np.empty((4, 2048, 2048), dtype=np.float32)
    for c in range(8):
        b, h = c // 2, c % 2
        outp[b, 1024 * h:1024 * (h + 1), :] = res.results[c]["outx"].T
    return outp
